# version 07
# baseline (speedup 1.0000x reference)
import numpy as np
import ml_dtypes
import concourse.bass as bass
import concourse.mybir as mybir
from concourse.bass_utils import run_bass_kernel_spmd
from contextlib import ExitStack

"""Shared builder library for the macaron megakernel (dev copy; inlined into kernel.py later)."""

F32 = mybir.dt.float32
BF16 = mybir.dt.bfloat16
AF = mybir.ActivationFunctionType
ALU = mybir.AluOpType
AX = mybir.AxisListType
BF = ml_dtypes.bfloat16

D = 2048
NC_D = 16
T = 1024
H = 8192
F = 1024
E = 16
SH = 2
NCORE = 8
EPS = 1e-5
RSQ_H = float(1.0 / np.sqrt(H))


def host_consts():
    idn = np.eye(128).astype(BF)
    meanw = np.full((128, 128), 1.0 / D, dtype=np.float32).astype(BF)
    sel0 = np.zeros((128, 128), dtype=BF)
    sel0[0, :] = 1.0
    ones_bf = np.ones((128, 128), dtype=BF)
    ind16 = np.zeros((16, 16 * 128), dtype=BF)
    for e in range(16):
        ind16[e, e * 128:(e + 1) * 128] = 1.0
    iota1 = np.broadcast_to(np.arange(1, T + 1, dtype=np.float32), (128, T)).copy()
    ones_f32 = np.ones((128, 1024), dtype=np.float32)
    causal = np.zeros((128, 128), dtype=np.float32)
    for q in range(128):
        causal[q, q + 1:] = -1e30
    h128 = hadamard(128).astype(BF)
    h64pad = np.zeros((64, 64), dtype=BF)
    h64pad[:, :] = hadamard(64).astype(BF)
    return dict(c_idn=idn, c_meanw=meanw, c_sel0=sel0, c_onesb=ones_bf,
                c_ind16=ind16, c_iota1=iota1, c_ones=ones_f32, c_causal=causal,
                c_h128=h128, c_h64=h64pad)


def hadamard(n):
    h = np.array([[1.0]])
    while h.shape[0] < n:
        h = np.block([[h, h], [h, -h]])
    return h


ENGINES = ("sync", "tensor", "vector", "scalar", "gpsimd")


class Sem:
    def __init__(self, handle):
        self.h = handle
        self.n = 0

    def plan(self, k=1):
        self.n += k
        return self.n


class Prog:
    def __init__(self, nc, ctx):
        self.nc = nc
        self.ctx = ctx
        self.q = {e: [] for e in ENGINES}
        self._semi = 0
        self._sbi = 0
        self._psi = 0

    def sem(self, name=None):
        self._semi += 1
        h = self.ctx.enter_context(self.nc.semaphore(name or f"s{self._semi}"))
        return Sem(h)

    def sb(self, shape, dtype, name=None):
        self._sbi += 1
        return self.ctx.enter_context(self.nc.sbuf_tensor(name or f"b{self._sbi}", shape, dtype))

    def ps(self, shape, dtype=F32, name=None):
        self._psi += 1
        return self.ctx.enter_context(self.nc.psum_tensor(name or f"p{self._psi}", shape, dtype))

    def on(self, eng, fn):
        self.q[eng].append(fn)

    def wait(self, eng, sem, val=None):
        v = sem.n if val is None else val
        if v > 0:
            self.q[eng].append(lambda h, s=sem, v=v: h.wait_ge(s.h, v))

    def emit(self):
        with self.nc.Block() as block:
            for eng in ENGINES:
                ops = self.q[eng]
                if not ops:
                    continue

                def body(h, ops=ops):
                    for fn in ops:
                        fn(h)
                getattr(block, eng)(body)

    def dma(self, qeng, out, in_, sem, **kw):
        c = sem.plan(16)
        self.q[qeng].append(lambda h, o=out, i=in_, s=sem: h.dma_start(out=o, in_=i, **kw).then_inc(s.h, 16))
        return c


"""Reusable emit-builders for the macaron kernel.

Convention: feature-major activations [d-on-partitions (chunks of 128), tokens-free].
Residual stream lives in HBM [D, T] f32 between sublayers; each sublayer streams it.
All semaphore increments ride on real instructions (never nops).

SBUF arenas (per-partition bytes):
  H64  [128, 32768] bf16 (64KB): prep scratch / LN xb+xq / ffn hidden / zT f32 / MoE acc f32
  M72  [128, 36864] bf16 (72KB): ffn weight bands + dwsl | MoE wg/ww1/ww2h/y | scan scratch f32
"""

W_T = 512
NT = T // W_T
HB = 512
NHB = H // HB
NHC = H // 128
Fdim = 1024
RSQ_DK = float(1.0 / np.sqrt(128.0))

WU1A, WU1B = 0, 8192
WU3A, WU3B = 16384, 24576
DW0, DW1 = 32768, 33792
WGA, WGB = 0, 4096
W1A, W1B = 8192, 12288
W2A, W2B = 16384, 20480
YBUF = 24576
SPARE = 32768


class MK:
    def __init__(self, nc, ctx, cb_aps, skip_consts=()):
        self.nc = nc
        self.skip_consts = set(skip_consts)
        p = self.p = Prog(nc, ctx)
        sb, ps = p.sb, p.ps
        self.c_idn = sb([128, 128], BF16)
        self.c_meanw = sb([128, 128], BF16)
        self.c_h128 = sb([128, 128], BF16) if "c_h128" not in self.skip_consts else None
        self.c_h64 = sb([64, 64], BF16) if "c_h64" not in self.skip_consts else None
        self.c_ind16 = sb([16, 16 * 128], BF16) if "c_ind16" not in self.skip_consts else None
        self.c_ones = sb([128, 1024], F32) if "c_ones" not in self.skip_consts else None
        self.c_iota1 = sb([128, 1024], F32) if "c_iota1" not in self.skip_consts else None
        self.c_causal = sb([128, 128], F32) if "c_causal" not in self.skip_consts else None
        self.H64 = sb([128, 32768], BF16)
        self.zf = self.H64.ap().bitcast(F32)
        self.M72 = sb([128, 36864], BF16)
        self.mf = self.M72.ap().bitcast(F32)
        self.xn = sb([128, NC_D * T], BF16)
        self.sg_t = sb([128, W_T], BF16)
        self.rinv = sb([128, W_T], F32)
        self.smu = sb([128, W_T], F32)
        self.svar = sb([128, W_T], F32)
        self.tmpa = sb([128, W_T], F32)
        self.x_tl = sb([128, W_T], F32)
        self.st_tl = sb([128, W_T], F32)
        self.eps_col = sb([128, 1], F32)
        self.lam_sb = sb([1, 16], F32)
        self.lamk = [sb([128, 1], F32) for _ in range(4)]
        self.etak = [sb([128, 1], F32) for _ in range(4)]
        self.lnlamk = [sb([128, 1], F32) for _ in range(4)]
        self.S_sb = sb([128, 64], F32)
        self.wab = sb([128, 256], BF16)
        self.rgbR = sb([128, 128], F32)
        self.rb1R = sb([128, 128], F32)
        self.sgbR = sb([128, 16], F32)
        self.sb1R = sb([128, 16], F32)
        self.cT = sb([16, 1024], BF16)
        self.cbuf = sb([128, W_T], F32)
        self.ga_t = sb([128, W_T], BF16)
        self.bc_t = sb([128, W_T], F32)
        self.ones2 = sb([2, 512], BF16)
        self.aff = sb([128, 16], F32)
        self.af2 = sb([128, 16], F32)
        self.m1c = sb([128, 4], F32)
        self.nm_col = sb([128, 1], F32)
        self.l_col = sb([128, 1], F32)
        self.rl_col = sb([128, 1], F32)
        self.pb = [ps([128, 512], F32) for _ in range(8)]
        self.cb = cb_aps

        self.sw = p.sem("sw")
        self.sld = p.sem("sld")
        self.spe = p.sem("spe")
        self.sve = p.sem("sve")
        self.sse = p.sem("sse")
        self.sst = p.sem("sst")
        self.sgp = p.sem("sgp")

    def wu1(self, i, a, b):
        o = WU1A if i == 0 else WU1B
        return self.M72[:, o + a:o + b]

    def wu3(self, i, a, b):
        o = WU3A if i == 0 else WU3B
        return self.M72[:, o + a:o + b]

    def dwsl(self, i, a=0, b=1024):
        o = DW0 if i == 0 else DW1
        return self.M72[:, o + a:o + b]

    def emit_consts(self):
        p, cb = self.p, self.cb
        for dst, key in [(self.c_idn, "c_idn"), (self.c_meanw, "c_meanw"),
                         (self.c_h128, "c_h128"), (self.c_h64, "c_h64"),
                         (self.c_ind16, "c_ind16"), (self.c_ones, "c_ones"),
                         (self.c_iota1, "c_iota1"), (self.c_causal, "c_causal")]:
            if key in cb and dst is not None:
                p.dma("sync", dst[:, :], cb[key][:, :], self.sld)
        self.consts_done = self.sld.n
        p.wait("vector", self.sld, self.consts_done)
        self.sve.plan(1)
        p.on("vector", lambda h: h.memset(self.eps_col[:, :], EPS).then_inc(self.sve.h, 1))
        self.eps_done = self.sve.n

    def cast_weight(self, dst_dram, src_ap, nsplit=8):
        p = self.p
        rows = src_ap.shape[0]
        step = rows // nsplit
        for j in range(nsplit):
            p.dma("gpsimd", dst_dram.ap()[j * step:(j + 1) * step, :],
                  src_ap[j * step:(j + 1) * step, :], self.sw)
        return self.sw.n

    def emit_lamprep(self, lam_logit_ap, eta_ap, lam_d, need_ln=False):
        p, sve, sse, sld = self.p, self.sve, self.sse, self.sld
        p.dma("sync", self.lam_sb[0:1, 0:4], lam_logit_ap[:, :], sld)
        ll = sld.n
        p.wait("scalar", sld, ll)
        sse.plan(1)
        p.on("scalar", lambda h: h.activation(
            self.lam_sb[0:1, 4:8], self.lam_sb[0:1, 0:4], AF.Sigmoid).then_inc(sse.h, 1))
        p.wait("vector", sse, sse.n)
        sve.plan(2)
        p.on("vector", lambda h: h.tensor_scalar_max(
            self.lam_sb[0:1, 4:8], self.lam_sb[0:1, 4:8], 1e-30).then_inc(sve.h, 1))
        p.on("vector", lambda h: h.tensor_scalar_min(
            self.lam_sb[0:1, 4:8], self.lam_sb[0:1, 4:8], 1.0 - 1e-7).then_inc(sve.h, 1))
        clamp_done = sve.n
        if need_ln:
            p.wait("scalar", sve, clamp_done)
            sse.plan(1)
            p.on("scalar", lambda h: h.activation(
                self.lam_sb[0:1, 8:12], self.lam_sb[0:1, 4:8], AF.Ln).then_inc(sse.h, 1))
            p.wait("sync", sse, sse.n)
        else:
            p.wait("sync", sve, clamp_done)
        p.dma("sync", lam_d.ap()[:, :], self.lam_sb[0:1, 0:16], sld)
        p.wait("sync", sld, sld.n)
        for k in range(4):
            p.dma("sync", self.lamk[k][:, :], lam_d.ap()[0:1, 4 + k:5 + k].broadcast_to([128, 1]), sld)
        for k in range(4):
            p.dma("sync", self.etak[k][:, :], eta_ap[0:1, k:k + 1].broadcast_to([128, 1]), sld)
        if need_ln:
            for k in range(4):
                p.dma("sync", self.lnlamk[k][:, :], lam_d.ap()[0:1, 8 + k:9 + k].broadcast_to([128, 1]), sld)
        return sld.n

    def emit_w2prep(self, w2bf, w2a, w2p, w2bf_done):
        p = self.p
        spe, sve, sld, sw = self.spe, self.sve, self.sld, self.sw
        p.wait("sync", spe, spe.n)
        p.wait("sync", sve, sve.n)
        H64 = self.H64
        w2tsl = lambda: H64[:, 16384:18432]
        prep_e = lambda: H64[:, 18432:20480]
        st2_in = lambda: H64[0:64, 0:8192]
        st2_e = lambda: H64[0:64, 8192:16384]
        prep_ve = 0
        for c in range(NHC):
            p.wait("sync", sw, w2bf_done)
            ld = p.dma("sync", w2tsl(), w2bf.ap()[:, c * 128:(c + 1) * 128], sld, transpose=True)
            p.wait("tensor", sld, ld)
            p.wait("tensor", sld, self.consts_done)
            for j in range(4):
                if prep_ve > 0:
                    p.wait("tensor", sve, prep_ve)
                spe.plan(1)
                p.on("tensor", lambda h, j=j: h.matmul(
                    self.pb[4][:, :], self.c_h128[:, :], w2tsl()[:, j * 512:(j + 1) * 512],
                    start=True, stop=True).then_inc(spe.h, 1))
                p.wait("vector", spe, spe.n)
                sve.plan(1)
                p.on("vector", lambda h, j=j: h.tensor_copy(
                    prep_e()[:, j * 512:(j + 1) * 512], self.pb[4][:, :]).then_inc(sve.h, 1))
                prep_ve = sve.n
            p.wait("sync", sve, prep_ve)
            p.dma("sync", w2a.ap()[c * 128:(c + 1) * 128, :], prep_e(), sld)
        st1_done = sld.n
        Aview = w2a.ap().rearrange("(c p) d -> c (p d)", c=64)
        Bview = w2p.ap().rearrange("(c p) d -> c (p d)", c=64)
        st2_ve = 0
        for blk in range(32):
            p.wait("sync", sld, st1_done)
            if st2_ve:
                p.wait("sync", sve, st2_ve)
            ld = p.dma("sync", st2_in(), Aview[:, blk * 8192:(blk + 1) * 8192], sld)
            p.wait("tensor", sld, ld)
            for j in range(16):
                if st2_ve:
                    p.wait("tensor", sve, st2_ve)
                spe.plan(1)
                p.on("tensor", lambda h, j=j: h.matmul(
                    self.pb[5][0:64, :], self.c_h64[:, :], st2_in()[:, j * 512:(j + 1) * 512],
                    start=True, stop=True).then_inc(spe.h, 1))
                p.wait("vector", spe, spe.n)
                sve.plan(1)
                p.on("vector", lambda h, j=j: h.tensor_scalar_mul(
                    st2_e()[:, j * 512:(j + 1) * 512], self.pb[5][0:64, :], RSQ_H).then_inc(sve.h, 1))
                st2_ve = sve.n
            p.wait("sync", sve, st2_ve)
            p.dma("sync", Bview[:, blk * 8192:(blk + 1) * 8192], st2_e(), sld)
        return sld.n

    def emit_ln(self, src_hbm, out_sb=None, affine=None, out_f32_hbm=None):
        p = self.p
        spe, sve, sse, sld = self.spe, self.sve, self.sse, self.sld
        pb, H64 = self.pb, self.H64
        mf = self.mf
        out_sb = out_sb if out_sb is not None else self.xn
        XB0, XQ0 = 0, 16384
        done = 0
        src_re = src_hbm.rearrange("(c p) t -> p c t", p=128)
        for tt in range(NT):
            t0 = tt * W_T
            p.wait("sync", spe, spe.n)
            p.wait("sync", sve, sve.n)
            p.wait("sync", sse, sse.n)
            xv = mf[:, 0:8192].rearrange("p (c t) -> p c t", c=16)
            ld = p.dma("sync", xv, src_re[:, :, t0:t0 + W_T], sld)
            p.wait("vector", sld, ld)
            p.wait("scalar", sld, ld)
            for c in range(NC_D):
                sve.plan(1)
                p.on("vector", lambda h, c=c, t0=t0: h.tensor_copy(
                    H64[:, XB0 + c * T + t0:XB0 + c * T + t0 + W_T],
                    mf[:, c * W_T:(c + 1) * W_T]).then_inc(sve.h, 1))
                sse.plan(1)
                p.on("scalar", lambda h, c=c, t0=t0: h.activation(
                    H64[:, XQ0 + c * T + t0:XQ0 + c * T + t0 + W_T],
                    mf[:, c * W_T:(c + 1) * W_T], AF.Square).then_inc(sse.h, 1))
            cast_ve, cast_se = sve.n, sse.n
            p.wait("tensor", sve, cast_ve)
            p.wait("tensor", sse, cast_se)
            for c in range(NC_D):
                spe.plan(1)
                o = XB0 + c * T + t0
                p.on("tensor", lambda h, o=o, c=c: h.matmul(
                    pb[0][:, :], self.c_meanw[:, :], H64[:, o:o + W_T],
                    start=(c == 0), stop=(c == NC_D - 1)).then_inc(spe.h, 1))
            for c in range(NC_D):
                spe.plan(1)
                o2 = XQ0 + c * T + t0
                p.on("tensor", lambda h, o2=o2, c=c: h.matmul(
                    pb[1][:, :], self.c_meanw[:, :], H64[:, o2:o2 + W_T],
                    start=(c == 0), stop=(c == NC_D - 1)).then_inc(spe.h, 1))
            mm_done = spe.n
            p.wait("vector", spe, mm_done)
            sve.plan(3)
            p.on("vector", lambda h: h.tensor_copy(self.smu[:, :], pb[0][:, :]).then_inc(sve.h, 1))
            p.on("vector", lambda h: h.tensor_mul(self.tmpa[:, :], pb[0][:, :], self.smu[:, :]).then_inc(sve.h, 1))
            p.on("vector", lambda h: h.tensor_sub(self.svar[:, :], pb[1][:, :], self.tmpa[:, :]).then_inc(sve.h, 1))
            v_stats = sve.n
            p.wait("scalar", sve, v_stats)
            p.wait("scalar", sve, self.eps_done)
            sse.plan(1)
            p.on("scalar", lambda h: h.activation(
                self.smu[:, :], self.svar[:, :], AF.Sqrt, bias=self.eps_col[:, 0:1]).then_inc(sse.h, 1))
            s_sqrt = sse.n
            p.wait("vector", sse, s_sqrt)
            sve.plan(1)
            p.on("vector", lambda h: h.reciprocal(self.rinv[:, :], self.smu[:, :]).then_inc(sve.h, 1))
            p.wait("vector", sve, sve.n)
            for c in range(NC_D):
                if affine is None and out_f32_hbm is None:
                    sve.plan(2)
                    p.on("vector", lambda h, c=c: h.tensor_sub(
                        self.tmpa[:, :], mf[:, c * W_T:(c + 1) * W_T], pb[0][:, :]).then_inc(sve.h, 1))
                    p.on("vector", lambda h, c=c, t0=t0: h.tensor_mul(
                        out_sb[:, c * T + t0:c * T + t0 + W_T], self.tmpa[:, :], self.rinv[:, :]).then_inc(sve.h, 1))
                    done = sve.n
                else:
                    sve.plan(2)
                    p.on("vector", lambda h, c=c: h.tensor_sub(
                        self.tmpa[:, :], mf[:, c * W_T:(c + 1) * W_T], pb[0][:, :]).then_inc(sve.h, 1))
                    if affine is None:
                        p.on("vector", lambda h: h.tensor_mul(
                            self.st_tl[:, :], self.tmpa[:, :], self.rinv[:, :]).then_inc(sve.h, 1))
                    else:
                        g_sb, b_sb = affine
                        p.on("vector", lambda h: h.tensor_mul(
                            self.tmpa[:, :], self.tmpa[:, :], self.rinv[:, :]).then_inc(sve.h, 1))
                        sve.plan(1)
                        tgt = self.st_tl if out_f32_hbm is not None else None
                        if out_f32_hbm is not None:
                            p.on("vector", lambda h, c=c: h.tensor_scalar(
                                self.st_tl[:, :], self.tmpa[:, :], g_sb[:, c:c + 1], b_sb[:, c:c + 1],
                                op0=ALU.mult, op1=ALU.add).then_inc(sve.h, 1))
                        else:
                            p.on("vector", lambda h, c=c, t0=t0: h.tensor_scalar(
                                out_sb[:, c * T + t0:c * T + t0 + W_T], self.tmpa[:, :],
                                g_sb[:, c:c + 1], b_sb[:, c:c + 1],
                                op0=ALU.mult, op1=ALU.add).then_inc(sve.h, 1))
                    done = sve.n
                    if out_f32_hbm is not None:
                        p.wait("sync", sve, done)
                        p.dma("sync", out_f32_hbm[c * 128:(c + 1) * 128, t0:t0 + W_T], self.st_tl[:, :], self.sst)
                        p.wait("vector", self.sst, self.sst.n)
        return done

    def emit_ffn(self, x_in_hbm, x_out_hbm, w1bf, w3bf, w2p, w1_done, w3_done, w2p_done):
        p = self.p
        spe, sve, sse, sld, sw, sst = self.spe, self.sve, self.sse, self.sld, self.sw, self.sst
        p.wait("sync", spe, spe.n)
        p.wait("sync", sve, sve.n)
        pb, H64, xn = self.pb, self.H64, self.xn
        p.wait("sync", sw, w1_done)
        p.wait("sync", sw, w3_done)
        band_mm_end = {}
        for tt in range(NT):
            t0 = tt * W_T
            hid_ve = 0
            ln_ve = sve.n
            for b in range(NHB):
                g = tt * NHB + b
                bi = g % 2
                h0 = b * HB
                if g >= 2:
                    p.wait("sync", spe, band_mm_end[g - 2])
                for dc in range(NC_D):
                    p.dma("sync", self.wu1(bi, dc * HB, (dc + 1) * HB),
                          w1bf.ap()[h0:h0 + HB, dc * 128:(dc + 1) * 128], sld, transpose=True)
                for dc in range(NC_D):
                    p.dma("sync", self.wu3(bi, dc * HB, (dc + 1) * HB),
                          w3bf.ap()[h0:h0 + HB, dc * 128:(dc + 1) * 128], sld, transpose=True)
                wband = sld.n
                p.wait("tensor", sld, wband)
                p.wait("tensor", sve, ln_ve)
                for ht in range(HB // 128):
                    pg, pl = (2, 3) if ht % 2 == 0 else (6, 7)
                    for c in range(NC_D):
                        spe.plan(1)
                        p.on("tensor", lambda h, c=c, ht=ht, t0=t0, bi=bi, pg=pg: h.matmul(
                            pb[pg][:, :], self.wu1(bi, c * HB + ht * 128, c * HB + ht * 128 + 128),
                            xn[:, c * T + t0:c * T + t0 + W_T],
                            start=(c == 0), stop=(c == NC_D - 1)).then_inc(spe.h, 1))
                    gate_done = spe.n
                    for c in range(NC_D):
                        spe.plan(1)
                        p.on("tensor", lambda h, c=c, ht=ht, t0=t0, bi=bi, pl=pl: h.matmul(
                            pb[pl][:, :], self.wu3(bi, c * HB + ht * 128, c * HB + ht * 128 + 128),
                            xn[:, c * T + t0:c * T + t0 + W_T],
                            start=(c == 0), stop=(c == NC_D - 1)).then_inc(spe.h, 1))
                    lin_done = spe.n
                    hcix = b * (HB // 128) + ht
                    p.wait("scalar", spe, gate_done)
                    sse.plan(1)
                    p.on("scalar", lambda h, pg=pg, hcix=hcix: h.activation(
                        self.sg_t[:, :], pb[pg][:, :], AF.Silu).then_inc(sse.h, 1))
                    s_silu = sse.n
                    p.wait("vector", sse, s_silu)
                    p.wait("vector", spe, lin_done)
                    sve.plan(1)
                    p.on("vector", lambda h, hcix=hcix, pl=pl: h.tensor_mul(
                        H64[:, hcix * W_T:(hcix + 1) * W_T], self.sg_t[:, :], pb[pl][:, :]).then_inc(sve.h, 1))
                    hid_ve = sve.n
                    if ht % 2 == 1 or ht == HB // 128 - 1:
                        p.wait("tensor", sse, s_silu)
                        p.wait("tensor", sve, hid_ve)
                band_mm_end[g] = spe.n
            p.wait("sync", sld, w2p_done)
            for half in range(2):
                d0h = half * 8
                ev_prev = sve.n
                mm_after = {}
                for hcix in range(NHC):
                    if hcix >= 2:
                        p.wait("sync", spe, mm_after[hcix - 2])
                    ld = p.dma("sync", self.dwsl(hcix % 2, 0, 1024),
                               w2p.ap()[hcix * 128:(hcix + 1) * 128, half * 1024:(half + 1) * 1024], sld)
                    p.wait("tensor", sld, ld)
                    if hcix == 0:
                        p.wait("tensor", sve, hid_ve)
                        p.wait("tensor", sve, ev_prev)
                    for dt in range(8):
                        spe.plan(1)
                        p.on("tensor", lambda h, hcix=hcix, dt=dt: h.matmul(
                            pb[dt][:, :], self.dwsl(hcix % 2, dt * 128, (dt + 1) * 128),
                            H64[:, hcix * W_T:(hcix + 1) * W_T],
                            start=(hcix == 0), stop=(hcix == NHC - 1)).then_inc(spe.h, 1))
                    mm_after[hcix] = spe.n
                dn_done = spe.n
                for dt in range(8):
                    dglob = d0h + dt
                    ld = p.dma("sync", self.x_tl[:, :], x_in_hbm[dglob * 128:(dglob + 1) * 128, t0:t0 + W_T], sld)
                    p.wait("vector", sld, ld)
                    p.wait("vector", spe, dn_done)
                    sve.plan(1)
                    p.on("vector", lambda h, dt=dt: h.scalar_tensor_tensor(
                        self.st_tl[:, :], pb[dt][:, :], 0.5, self.x_tl[:, :],
                        op0=ALU.mult, op1=ALU.add).then_inc(sve.h, 1))
                    ev = sve.n
                    p.wait("sync", sve, ev)
                    p.dma("sync", x_out_hbm[dglob * 128:(dglob + 1) * 128, t0:t0 + W_T], self.st_tl[:, :], sst)
                    p.wait("vector", sst, sst.n)
        p.wait("sync", sst, sst.n)
        return sst.n

    def emit_zscan(self, wzbf, wzbf_done, hT_o, S_o):
        p = self.p
        spe, sve, sld, sw, sst = self.spe, self.sve, self.sld, self.sw, self.sst
        p.wait("sync", spe, spe.n)
        p.wait("sync", sve, sve.n)
        pb, xn, zf, mf = self.pb, self.xn, self.zf, self.mf
        ln2_done = sve.n
        for tt in range(NT):
            t0 = tt * W_T
            for q in range(4):
                e0 = q * 4
                p.wait("sync", sw, wzbf_done)
                p.wait("sync", sve, sve.n)
                for dc in range(NC_D):
                    p.dma("sync", self.wu1(0, dc * HB, (dc + 1) * HB),
                          wzbf.ap()[q * 512:(q + 1) * 512, dc * 128:(dc + 1) * 128], sld, transpose=True)
                wz_ld = sld.n
                p.wait("tensor", sld, wz_ld)
                p.wait("tensor", sve, ln2_done)
                for et in range(4):
                    for c in range(NC_D):
                        spe.plan(1)
                        p.on("tensor", lambda h, c=c, et=et, t0=t0: h.matmul(
                            pb[et][:, :], self.wu1(0, c * HB + et * 128, c * HB + et * 128 + 128),
                            xn[:, c * T + t0:c * T + t0 + W_T],
                            start=(c == 0), stop=(c == NC_D - 1)).then_inc(spe.h, 1))
                z_done = spe.n
                p.wait("vector", spe, z_done)
                for et in range(4):
                    eg = e0 + et
                    sve.plan(1)
                    p.on("vector", lambda h, eg=eg, t0=t0, et=et: h.tensor_copy(
                        zf[:, eg * T + t0:eg * T + t0 + W_T], pb[et][:, :]).then_inc(sve.h, 1))
                z_ev = sve.n
                p.wait("tensor", sve, z_ev)
        LAMT, MT, HCA, HCB = 0, 4096, 8192, 9216
        for k in range(4):
            sve.plan(1)
            p.on("vector", lambda h, k=k: h.tensor_scalar_mul(
                mf[:, LAMT + k * T:LAMT + (k + 1) * T], self.c_ones[:, 0:T],
                self.lamk[k][:, 0:1]).then_inc(sve.h, 1))
        for c in range(NC_D):
            for k in range(4):
                sve.plan(1)
                p.on("vector", lambda h, c=c, k=k: h.tensor_tensor_scan(
                    mf[:, MT + k * T:MT + (k + 1) * T], mf[:, LAMT + k * T:LAMT + (k + 1) * T],
                    zf[:, c * T:(c + 1) * T], 0.0, op0=ALU.mult, op1=ALU.add).then_inc(sve.h, 1))
            sve.plan(4)
            p.on("vector", lambda h, c=c: h.scalar_tensor_tensor(
                mf[:, HCA:HCA + T], mf[:, MT:MT + T], self.etak[0][:, 0:1],
                zf[:, c * T:(c + 1) * T], op0=ALU.mult, op1=ALU.add).then_inc(sve.h, 1))
            p.on("vector", lambda h: h.scalar_tensor_tensor(
                mf[:, HCB:HCB + T], mf[:, MT + T:MT + 2 * T], self.etak[1][:, 0:1],
                mf[:, HCA:HCA + T], op0=ALU.mult, op1=ALU.add).then_inc(sve.h, 1))
            p.on("vector", lambda h: h.scalar_tensor_tensor(
                mf[:, HCA:HCA + T], mf[:, MT + 2 * T:MT + 3 * T], self.etak[2][:, 0:1],
                mf[:, HCB:HCB + T], op0=ALU.mult, op1=ALU.add).then_inc(sve.h, 1))
            p.on("vector", lambda h, c=c: h.scalar_tensor_tensor(
                zf[:, c * T:(c + 1) * T], mf[:, MT + 3 * T:MT + 4 * T], self.etak[3][:, 0:1],
                mf[:, HCA:HCA + T], op0=ALU.mult, op1=ALU.add).then_inc(sve.h, 1))
            for k in range(4):
                sve.plan(1)
                p.on("vector", lambda h, c=c, k=k: h.tensor_copy(
                    self.S_sb[:, c * 4 + k:c * 4 + k + 1],
                    mf[:, MT + k * T + T - 1:MT + k * T + T]).then_inc(sve.h, 1))
            scan_done = sve.n
            p.wait("sync", sve, scan_done)
            p.dma("sync", hT_o[c * 128:(c + 1) * 128, :], zf[:, c * T:(c + 1) * T], sst)
        p.wait("sync", sve, sve.n)
        p.dma("sync", S_o[:, :], self.S_sb[:, :], sst)
        return sst.n

    def emit_carry_fix(self, hT_in, hfix_out, S_all_ap, expo_ap):
        p = self.p
        sve, sse, sld, sst = self.sve, self.sse, self.sld, self.sst
        p.wait("sync", self.spe, self.spe.n)
        p.wait("sync", sve, sve.n)
        mf = self.mf
        POW, HL, HCA, HCB, SG = 0, 4096, 5120, 6144, 7168
        er = p.sb([128, 8], F32)
        tcol = p.sb([128, 2], F32)
        cacc = p.sb([128, 128], F32)
        ceta = p.sb([128, 64], F32)
        for r in range(8):
            p.dma("sync", mf[:, SG + r * 64:SG + (r + 1) * 64], S_all_ap[r, :, :], sld)
        for r in range(8):
            p.dma("sync", er[:, r:r + 1], expo_ap[0:1, r:r + 1].broadcast_to([128, 1]), sld)
        lds = sld.n
        p.wait("vector", self.sld, lds)
        p.wait("scalar", self.sld, lds)
        for k in range(4):
            sve.plan(1)
            p.on("vector", lambda h, k=k: h.tensor_scalar_mul(
                mf[:, POW + k * T:POW + (k + 1) * T], self.c_iota1[:, 0:T],
                self.lnlamk[k][:, 0:1]).then_inc(sve.h, 1))
        p.wait("scalar", sve, sve.n)
        for k in range(4):
            sse.plan(1)
            p.on("scalar", lambda h, k=k: h.activation(
                mf[:, POW + k * T:POW + (k + 1) * T], mf[:, POW + k * T:POW + (k + 1) * T],
                AF.Exp).then_inc(sse.h, 1))
        pow_done = sse.n
        p.wait("vector", sse, pow_done)
        for k in range(4):
            sve.plan(1)
            p.on("vector", lambda h, k=k: h.memset(cacc[:, k * 32:k * 32 + 16], 0.0).then_inc(sve.h, 1))
            src_off = k * 32
            dst_off = k * 32 + 16
            for r in range(8):
                sve.plan(1)
                p.on("vector", lambda h, k=k, r=r: h.tensor_mul(
                    tcol[:, 0:1], self.lnlamk[k][:, 0:1], er[:, r:r + 1]).then_inc(sve.h, 1))
                p.wait("scalar", sve, sve.n)
                sse.plan(1)
                p.on("scalar", lambda h: h.activation(
                    tcol[:, 1:2], tcol[:, 0:1], AF.Exp).then_inc(sse.h, 1))
                p.wait("vector", sse, sse.n)
                sve.plan(1)
                Sg_slice = lambda hh, r=r, k=k: None
                p.on("vector", lambda h, k=k, r=r, s=src_off, d=dst_off: h.scalar_tensor_tensor(
                    cacc[:, d:d + 16],
                    mf[:, SG + r * 64 + k:SG + (r + 1) * 64:4],
                    tcol[:, 1:2], cacc[:, s:s + 16],
                    op0=ALU.mult, op1=ALU.add).then_inc(sve.h, 1))
                src_off, dst_off = dst_off, src_off
            sve.plan(1)
            p.on("vector", lambda h, k=k, s=src_off: h.tensor_scalar_mul(
                ceta[:, k * 16:(k + 1) * 16], cacc[:, s:s + 16], self.etak[k][:, 0:1]).then_inc(sve.h, 1))
        ceta_done = sve.n
        for c in range(NC_D):
            ld = p.dma("sync", mf[:, HL:HL + T], hT_in[c * 128:(c + 1) * 128, :], sld)
            p.wait("vector", sld, ld)
            p.wait("vector", sse, pow_done)
            sve.plan(4)
            p.on("vector", lambda h, c=c: h.scalar_tensor_tensor(
                mf[:, HCA:HCA + T], mf[:, POW:POW + T], ceta[:, 0 * 16 + c:0 * 16 + c + 1],
                mf[:, HL:HL + T], op0=ALU.mult, op1=ALU.add).then_inc(sve.h, 1))
            p.on("vector", lambda h, c=c: h.scalar_tensor_tensor(
                mf[:, HCB:HCB + T], mf[:, POW + T:POW + 2 * T], ceta[:, 1 * 16 + c:1 * 16 + c + 1],
                mf[:, HCA:HCA + T], op0=ALU.mult, op1=ALU.add).then_inc(sve.h, 1))
            p.on("vector", lambda h, c=c: h.scalar_tensor_tensor(
                mf[:, HCA:HCA + T], mf[:, POW + 2 * T:POW + 3 * T], ceta[:, 2 * 16 + c:2 * 16 + c + 1],
                mf[:, HCB:HCB + T], op0=ALU.mult, op1=ALU.add).then_inc(sve.h, 1))
            p.on("vector", lambda h, c=c: h.scalar_tensor_tensor(
                mf[:, HCB:HCB + T], mf[:, POW + 3 * T:POW + 4 * T], ceta[:, 3 * 16 + c:3 * 16 + c + 1],
                mf[:, HCA:HCA + T], op0=ALU.mult, op1=ALU.add).then_inc(sve.h, 1))
            fx = sve.n
            p.wait("sync", sve, fx)
            p.dma("sync", hfix_out[c * 128:(c + 1) * 128, :], mf[:, HCB:HCB + T], sst)
            p.wait("vector", sst, sst.n)
        p.wait("sync", sst, sst.n)
        return sst.n

    def emit_add(self, a_hbm, b_hbm, out_hbm):
        p = self.p
        sve, sld, sst = self.sve, self.sld, self.sst
        for c in range(NC_D):
            for tt in range(NT):
                t0 = tt * W_T
                la = p.dma("sync", self.x_tl[:, :], a_hbm[c * 128:(c + 1) * 128, t0:t0 + W_T], sld)
                lb = p.dma("sync", self.st_tl[:, :], b_hbm[c * 128:(c + 1) * 128, t0:t0 + W_T], sld)
                p.wait("vector", sld, lb)
                sve.plan(1)
                p.on("vector", lambda h: h.tensor_add(
                    self.st_tl[:, :], self.st_tl[:, :], self.x_tl[:, :]).then_inc(sve.h, 1))
                ev = sve.n
                p.wait("sync", sve, ev)
                p.dma("sync", out_hbm[c * 128:(c + 1) * 128, t0:t0 + W_T], self.st_tl[:, :], sst)
                p.wait("vector", sst, sst.n)
        p.wait("sync", sst, sst.n)
        return sst.n

    def emit_proj(self, wbf, wbf_done, out_hbm, bias_sb=None, scale=None, xn_src=None,
                  add_hbm=None):
        p = self.p
        spe, sve, sse, sld, sst, sw = self.spe, self.sve, self.sse, self.sld, self.sst, self.sw
        p.wait("sync", spe, spe.n)
        p.wait("sync", sve, sve.n)
        pb = self.pb
        mf = self.mf
        M72 = self.M72
        stage_bf = lambda ot: M72[:, ot * 512:(ot + 1) * 512]
        stage_f32 = lambda ot: mf[:, 4096 + ot * 512:4096 + (ot + 1) * 512]
        xcache = mf[:, 8192 + 0:8192 + 4096]
        xn = xn_src if xn_src is not None else self.xn
        p.wait("sync", sw, wbf_done)
        if add_hbm is not None:
            add_re = add_hbm.rearrange("(c p) t -> p c t", p=128)
        out_re = out_hbm.rearrange("(c p) t -> p c t", p=128)
        for tt in range(NT):
            t0 = tt * W_T
            for grp in range(2):
                se_prev = sse.n
                ve_prev = sve.n
                st_prev = sst.n
                mm_after = {}
                for dc in range(NC_D):
                    if dc >= 2:
                        p.wait("sync", spe, mm_after[dc - 2])
                    ld = p.dma("sync", self.dwsl(dc % 2, 0, 1024),
                               wbf.ap()[dc * 128:(dc + 1) * 128, grp * 1024:(grp + 1) * 1024], sld)
                    p.wait("tensor", sld, ld)
                    if dc == 0:
                        p.wait("tensor", sse, se_prev)
                        p.wait("tensor", sve, ve_prev)
                    for ot in range(8):
                        spe.plan(1)
                        p.on("tensor", lambda h, dc=dc, ot=ot, t0=t0: h.matmul(
                            pb[ot][:, :], self.dwsl(dc % 2, ot * 128, (ot + 1) * 128),
                            xn[:, dc * T + t0:dc * T + t0 + W_T],
                            start=(dc == 0), stop=(dc == NC_D - 1)).then_inc(spe.h, 1))
                    mm_after[dc] = spe.n
                mm_done = spe.n
                if add_hbm is None:
                    p.wait("scalar", spe, mm_done)
                    p.wait("scalar", sst, st_prev)
                    for ot in range(8):
                        og = grp * 8 + ot
                        sse.plan(1)
                        bias_col = bias_sb[:, og:og + 1] if bias_sb is not None else 0.0
                        sc = scale if scale is not None else 1.0
                        p.on("scalar", lambda h, ot=ot, bias_col=bias_col, sc=sc: h.activation(
                            stage_bf(ot), pb[ot][:, :], AF.Identity, bias=bias_col,
                            scale=sc).then_inc(sse.h, 1))
                    ev = sse.n
                    p.wait("sync", sse, ev)
                    for ot in range(8):
                        og = grp * 8 + ot
                        p.dma("sync", out_hbm[og * 128:(og + 1) * 128, t0:t0 + W_T], stage_bf(ot), sst)
                else:
                    ld = p.dma("sync", xcache.rearrange("p (c t) -> p c t", c=8),
                               add_re[:, grp * 8:(grp + 1) * 8, t0:t0 + W_T], sld)
                    p.wait("vector", sld, ld)
                    p.wait("vector", spe, mm_done)
                    p.wait("vector", sst, st_prev)
                    for ot in range(8):
                        og = grp * 8 + ot
                        bias_col = bias_sb[:, og:og + 1]
                        sve.plan(2)
                        p.on("vector", lambda h, ot=ot, bias_col=bias_col: h.tensor_scalar(
                            self.st_tl[:, :], pb[ot][:, :], bias_col, None,
                            op0=ALU.add).then_inc(sve.h, 1))
                        p.on("vector", lambda h, ot=ot: h.tensor_add(
                            stage_f32(ot), self.st_tl[:, :],
                            mf[:, 8192 + ot * 512:8192 + (ot + 1) * 512]).then_inc(sve.h, 1))
                    ev = sve.n
                    p.wait("sync", sve, ev)
                    for ot in range(8):
                        og = grp * 8 + ot
                        p.dma("sync", out_hbm[og * 128:(og + 1) * 128, t0:t0 + W_T], stage_f32(ot), sst)
        p.wait("sync", sst, sst.n)
        return sst.n

    def emit_moe(self, x_in_hbm, x_out_hbm, W, biasR, dbg=None):
        p = self.p
        spe, sve, sse, sld, sst, sw = self.spe, self.sve, self.sse, self.sld, self.sst, self.sw
        pb, xn, zf, M72 = self.pb, self.xn, self.zf, self.M72
        ACC = 0
        wab = self.wab
        rb2b = self.M72[0:16, 32768:34816]
        sb2b = self.M72[0:2, 34816:36864]
        f32st = self.mf[0:16, 0:2048]
        rgbR, rb1R, sgbR, sb1R = self.rgbR, self.rb1R, self.sgbR, self.sb1R
        cT, cbuf, ga_t, bc_t = self.cT, self.cbuf, self.ga_t, self.bc_t
        ones2, aff, af2, m1c = self.ones2, self.aff, self.af2, self.m1c

        gpsimd_gate = spe.n
        p.wait("gpsimd", spe, gpsimd_gate)
        p.wait("gpsimd", sve, sve.n)
        p.wait("sync", spe, spe.n)
        p.wait("sync", sve, sve.n)

        waf = self.mf[:, 2048:2304]
        self._moe_small_loads(W, biasR, waf, wab, rb2b, sb2b, f32st, rgbR, rb1R, sgbR, sb1R, ones2)

        p.wait("vector", sve, sve.n)
        for c in range(NC_D):
            sve.plan(1)
            p.on("vector", lambda h, c=c: h.tensor_copy(
                zf[:, ACC + c * T:ACC + (c + 1) * T], xn[:, c * T:(c + 1) * T]).then_inc(sve.h, 1))
        acc_init = sve.n

        p.wait("tensor", sve, acc_init)
        tr_ev = 0
        for tt in range(NT):
            for g4 in range(4):
                tq = tt * W_T + g4 * 128
                if tr_ev:
                    p.wait("tensor", sve, tr_ev)
                for dc in range(NC_D):
                    spe.plan(1)
                    p.on("tensor", lambda h, dc=dc, tq=tq: h.matmul(
                        pb[6][:, 0:16], xn[:, dc * T + tq:dc * T + tq + 128],
                        wab[:, dc * 16:(dc + 1) * 16],
                        start=(dc == 0), stop=(dc == NC_D - 1)).then_inc(spe.h, 1))
                aff_mm = spe.n
                p.wait("scalar", spe, aff_mm)
                sse.plan(1)
                p.on("scalar", lambda h: h.activation(
                    aff[:, :], pb[6][:, 0:16], AF.Sigmoid).then_inc(sse.h, 1))
                p.wait("vector", sse, sse.n)
                ops = [
                    lambda h: h.reduce_max(m1c[:, 0:1], aff[:, :], axis=AX.X),
                    lambda h: h.tensor_single_scalar(af2[:, :], aff[:, :], m1c[:, 0:1], op=ALU.is_equal),
                    lambda h: h.scalar_tensor_tensor(af2[:, :], af2[:, :], -1e30, aff[:, :], op0=ALU.mult, op1=ALU.add),
                    lambda h: h.reduce_max(m1c[:, 1:2], af2[:, :], axis=AX.X),
                    lambda h: h.tensor_add(m1c[:, 2:3], m1c[:, 0:1], m1c[:, 1:2]),
                    lambda h: h.reciprocal(m1c[:, 3:4], m1c[:, 2:3]),
                    lambda h: h.tensor_single_scalar(af2[:, :], aff[:, :], m1c[:, 1:2], op=ALU.is_ge),
                    lambda h: h.tensor_mul(af2[:, :], af2[:, :], aff[:, :]),
                    lambda h: h.tensor_scalar_mul(ga_t[:, 0:16], af2[:, :], m1c[:, 3:4]),
                ]
                for op_ in ops:
                    sve.plan(1)
                    p.on("vector", lambda h, op_=op_: op_(h).then_inc(sve.h, 1))
                    p.wait("vector", sve, sve.n)
                c_done = sve.n
                p.wait("tensor", sve, c_done)
                spe.plan(1)
                p.on("tensor", lambda h: h.matmul(
                    pb[7][0:16, 0:128], ga_t[:, 0:16], self.c_idn[:, :],
                    start=True, stop=True).then_inc(spe.h, 1))
                tp = spe.n
                p.wait("vector", spe, tp)
                sve.plan(1)
                p.on("vector", lambda h, tq=tq: h.tensor_copy(
                    cT[0:16, tq:tq + 128], pb[7][0:16, 0:128]).then_inc(sve.h, 1))
                tr_ev = sve.n

        if dbg is not None and "xn_d" in dbg:
            p.wait("sync", sve, sve.n)
            p.dma("sync", dbg["xn_d"][:, :], xn[:, :], sst)
            p.dma("sync", dbg["cT_d"][:, :], cT[0:16, :], sst)
            p.dma("sync", dbg["aff_d"][:, :], aff[:, :], sst)
            p.dma("sync", dbg["ctok_d"][:, :], ga_t[:, 0:16], sst)
            p.dma("sync", dbg["m1c_d"][:, :], m1c[:, :], sst)
        NB, NG = 4, 4
        WG = (WGA, WGB)
        W1B_ = (W1A, W1B)
        W2B_ = (W2A, W2B)
        experts = []
        for e in range(E + SH):
            routed = e < E
            eh = e if routed else e - E
            g3 = (W["rg"] if routed else W["sg"]).rearrange("e (dc p) f -> e p dc f", p=128)
            w13 = (W["rw1"] if routed else W["sw1"]).rearrange("e (dc p) f -> e p dc f", p=128)
            w23 = (W["rw2"] if routed else W["sw2"]).rearrange("e (fc p) d -> e p fc d", p=128)
            for band in range(NB):
                experts.append(("B", e, eh, routed, band, g3, w13))
            for grp in range(NG):
                experts.append(("G", e, eh, routed, grp, w23, None))
        band_ctr = [0]
        grp_ctr = [0]
        band_end = {}
        grp_end = {}
        cast_sem = {}

        def issue_cast(task_ix):
            kind, e_, eh_, routed_, j, a1, a2 = experts[task_ix]
            if kind == "B":
                n = band_ctr[0]
                band_ctr[0] += 1
                bi = n % 2
                if n >= 2:
                    p.wait("gpsimd", spe, band_end[n - 2])
                else:
                    p.wait("gpsimd", spe, spe.n)
                    p.wait("gpsimd", sve, sve.n)
                f0 = j * 256
                wgv = self.M72[:, WG[bi]:WG[bi] + 4096].rearrange("p (dc f) -> p dc f", dc=16)
                ww1v = self.M72[:, W1B_[bi]:W1B_[bi] + 4096].rearrange("p (dc f) -> p dc f", dc=16)
                p.dma("gpsimd", wgv, a1[eh_, :, :, f0:f0 + 256], self.sw)
                p.dma("gpsimd", ww1v, a2[eh_, :, :, f0:f0 + 256], self.sw)
            else:
                n = grp_ctr[0]
                grp_ctr[0] += 1
                gi = n % 2
                if n >= 2:
                    p.wait("gpsimd", spe, grp_end[n - 2])
                ww2v = self.M72[:, W2B_[gi]:W2B_[gi] + 4096].rearrange("p (fc d) -> p fc d", fc=8)
                p.dma("gpsimd", ww2v, a1[eh_, :, :, j * 512:(j + 1) * 512], self.sw)
            cast_sem[task_ix] = self.sw.n

        sgv = {}
        fc_ctr = [0]
        issue_cast(0)
        issue_cast(1)
        for k, task in enumerate(experts):
            kind, e_, eh_, routed_, j, a1, a2 = task
            p.wait("tensor", self.sw, cast_sem[k])
            if kind == "B":
                p.wait("tensor", sve, sve.n)
                bi = (sum(1 for kk in range(k) if experts[kk][0] == "B")) % 2
                gbR = rgbR if routed_ else sgbR
                b1R = rb1R if routed_ else sb1R
                for tt in range(NT):
                    t0 = tt * W_T
                    if routed_:
                        p.wait("tensor", sve, sve.n)
                        spe.plan(1)
                        p.on("tensor", lambda h, e_=e_, t0=t0: h.matmul(
                            pb[7][:, :], self.c_ind16[:, e_ * 128:(e_ + 1) * 128],
                            cT[0:16, t0:t0 + W_T], start=True, stop=True).then_inc(spe.h, 1))
                        p.wait("vector", spe, spe.n)
                        sve.plan(1)
                        p.on("vector", lambda h: h.tensor_copy(
                            cbuf[:, :], pb[7][:, :]).then_inc(sve.h, 1))
                    for ft in range(2):
                        fcl = j * 2 + ft
                        fg = fc_ctr[0]
                        fc_ctr[0] += 1
                        pg, pl = (2, 3) if fg % 2 == 0 else (5, 6)
                        if fg >= 2:
                            p.wait("tensor", sse, sgv[fg - 2][0])
                            p.wait("tensor", sve, sgv[fg - 2][1])
                        for dc in range(NC_D):
                            spe.plan(1)
                            p.on("tensor", lambda h, dc=dc, t0=t0, bi=bi, ft=ft, pg=pg: h.matmul(
                                pb[pg][:, :], self.M72[:, WG[bi] + dc * 256 + ft * 128:WG[bi] + dc * 256 + ft * 128 + 128],
                                xn[:, dc * T + t0:dc * T + t0 + W_T],
                                start=(dc == 0), stop=(dc == NC_D - 1)).then_inc(spe.h, 1))
                        a_done = spe.n
                        for dc in range(NC_D):
                            spe.plan(1)
                            p.on("tensor", lambda h, dc=dc, t0=t0, bi=bi, ft=ft, pl=pl: h.matmul(
                                pb[pl][:, :], self.M72[:, W1B_[bi] + dc * 256 + ft * 128:W1B_[bi] + dc * 256 + ft * 128 + 128],
                                xn[:, dc * T + t0:dc * T + t0 + W_T],
                                start=(dc == 0), stop=(dc == NC_D - 1)).then_inc(spe.h, 1))
                        b_done = spe.n
                        p.wait("scalar", spe, a_done)
                        sse.plan(1)
                        bcol = gbR[:, eh_ * 8 + fcl:eh_ * 8 + fcl + 1]
                        p.on("scalar", lambda h, bcol=bcol, pg=pg: h.activation(
                            ga_t[:, :], pb[pg][:, :], AF.Gelu, bias=bcol).then_inc(sse.h, 1))
                        s_g = sse.n
                        p.wait("vector", spe, b_done)
                        p.wait("vector", sse, s_g)
                        b1col = b1R[:, eh_ * 8 + fcl:eh_ * 8 + fcl + 1]
                        if routed_:
                            sve.plan(2)
                            p.on("vector", lambda h, b1col=b1col, pl=pl: h.scalar_tensor_tensor(
                                bc_t[:, :], pb[pl][:, :], b1col, cbuf[:, :],
                                op0=ALU.add, op1=ALU.mult).then_inc(sve.h, 1))
                        else:
                            sve.plan(2)
                            p.on("vector", lambda h, b1col=b1col, pl=pl: h.tensor_scalar(
                                bc_t[:, :], pb[pl][:, :], b1col, None, op0=ALU.add).then_inc(sve.h, 1))
                        p.on("vector", lambda h, fcl=fcl, t0=t0: h.tensor_mul(
                            self.M72[:, YBUF + fcl * 1024 + t0:YBUF + fcl * 1024 + t0 + W_T],
                            ga_t[:, :], bc_t[:, :]).then_inc(sve.h, 1))
                        sgv[fg] = (s_g, sve.n)
                nb_done = sum(1 for kk in range(k + 1) if experts[kk][0] == "B")
                band_end[nb_done - 1] = spe.n
            else:
                gi = (sum(1 for kk in range(k) if experts[kk][0] == "G")) % 2
                pbase = 0 if gi == 0 else 4
                for tt in range(NT):
                    t0 = tt * W_T
                    p.wait("tensor", sve, sve.n)
                    p.wait("tensor", sse, sse.n)
                    for fc in range(8):
                        for dt in range(4):
                            spe.plan(1)
                            p.on("tensor", lambda h, fc=fc, dt=dt, t0=t0, gi=gi, pbase=pbase: h.matmul(
                                pb[pbase + dt][:, :],
                                self.M72[:, W2B_[gi] + fc * 512 + dt * 128:W2B_[gi] + fc * 512 + dt * 128 + 128],
                                self.M72[:, YBUF + fc * 1024 + t0:YBUF + fc * 1024 + t0 + W_T],
                                start=(fc == 0), stop=(fc == 7)).then_inc(spe.h, 1))
                    o_done = spe.n
                    p.wait("vector", spe, o_done)
                    for dt in range(4):
                        dg = j * 4 + dt
                        sve.plan(1)
                        p.on("vector", lambda h, dt=dt, dg=dg, t0=t0, pbase=pbase: h.tensor_add(
                            zf[:, ACC + dg * T + t0:ACC + dg * T + t0 + W_T],
                            zf[:, ACC + dg * T + t0:ACC + dg * T + t0 + W_T],
                            pb[pbase + dt][:, :]).then_inc(sve.h, 1))
                ng_done = sum(1 for kk in range(k + 1) if experts[kk][0] == "G")
                grp_end[ng_done - 1] = spe.n
            if k + 2 < len(experts):
                issue_cast(k + 2)

        p.wait("vector", sve, sve.n)
        sve.plan(1)
        p.on("vector", lambda h: h.memset(ones2[:, :], 1.0).then_inc(sve.h, 1))
        ones_done = sve.n
        p.wait("tensor", sve, ones_done)
        for tt in range(NT):
            t0 = tt * W_T
            for dgrp in range(2):
                for dt in range(8):
                    dg = dgrp * 8 + dt
                    spe.plan(2)
                    p.on("tensor", lambda h, dg=dg, t0=t0, dt=dt: h.matmul(
                        pb[dt][:, :], self.M72[0:16, 32768 + dg * 128:32768 + (dg + 1) * 128],
                        cT[0:16, t0:t0 + W_T], start=True, stop=False).then_inc(spe.h, 1))
                    p.on("tensor", lambda h, dg=dg, t0=t0, dt=dt: h.matmul(
                        pb[dt][:, :], self.M72[0:2, 34816 + dg * 128:34816 + (dg + 1) * 128],
                        ones2[:, 0:W_T], start=False, stop=True).then_inc(spe.h, 1))
                bmm = spe.n
                p.wait("vector", spe, bmm)
                for dt in range(8):
                    dg = dgrp * 8 + dt
                    sve.plan(1)
                    p.on("vector", lambda h, dt=dt, dg=dg, t0=t0: h.tensor_add(
                        zf[:, ACC + dg * T + t0:ACC + dg * T + t0 + W_T],
                        zf[:, ACC + dg * T + t0:ACC + dg * T + t0 + W_T],
                        pb[dt][:, :]).then_inc(sve.h, 1))
                p.wait("tensor", sve, sve.n)

        acc_done = sve.n
        for c in range(NC_D):
            for tt in range(NT):
                t0 = tt * W_T
                ld = p.dma("sync", self.x_tl[:, :], x_in_hbm[c * 128:(c + 1) * 128, t0:t0 + W_T], sld)
                p.wait("vector", sld, ld)
                sve.plan(1)
                p.on("vector", lambda h, c=c, t0=t0: h.scalar_tensor_tensor(
                    self.st_tl[:, :], zf[:, ACC + c * T + t0:ACC + c * T + t0 + W_T], 0.5,
                    self.x_tl[:, :], op0=ALU.mult, op1=ALU.add).then_inc(sve.h, 1))
                ev = sve.n
                p.wait("sync", sve, ev)
                p.dma("sync", x_out_hbm[c * 128:(c + 1) * 128, t0:t0 + W_T], self.st_tl[:, :], sst)
                p.wait("vector", sst, sst.n)
        p.wait("sync", sst, sst.n)
        return sst.n

    def _moe_small_loads(self, W, biasR, waf, wab, rb2b, sb2b, f32st, rgbR, rb1R, sgbR, sb1R, ones2):
        p, sld, sve = self.p, self.sld, self.sve
        p.dma("sync", waf[:, :], W["wa_r"], sld)
        p.dma("sync", rgbR[:, :], biasR["rgb_r"], sld)
        p.dma("sync", rb1R[:, :], biasR["rb1_r"], sld)
        p.dma("sync", sgbR[:, :], biasR["sgb_r"], sld)
        p.dma("sync", sb1R[:, :], biasR["sb1_r"], sld)
        p.dma("sync", f32st, W["rb2"], sld)
        l1 = sld.n
        p.wait("vector", sld, l1)
        sve.plan(2)
        p.on("vector", lambda h: h.tensor_copy(wab[:, :], waf[:, :]).then_inc(sve.h, 1))
        p.on("vector", lambda h: h.tensor_copy(rb2b, f32st).then_inc(sve.h, 1))
        rb2c = sve.n
        p.wait("sync", sve, rb2c)
        p.dma("sync", self.mf[0:2, 0:2048], W["sb2"], sld)
        l2 = sld.n
        p.wait("vector", sld, l2)
        sve.plan(1)
        p.on("vector", lambda h: h.tensor_copy(sb2b, self.mf[0:2, 0:2048]).then_inc(sve.h, 1))

    def emit_attn(self, qt_my, kt_full, v_full, mask_in):
        p = self.p
        spe, sve, sse, sld, sst = self.spe, self.sve, self.sse, self.sld, self.sst
        pb, H64, M72, mf = self.pb, self.H64, self.M72, self.mf
        p.wait("sync", spe, spe.n)
        p.wait("sync", sve, sve.n)
        p.wait("sync", sse, sse.n)
        PB0 = 0
        AVT0 = 8192
        K10 = 24576
        MB0 = 16384
        V10 = 24576
        qh = M72[:, 32768:33792]
        pt_sb = [M72[:, 33792:34816], M72[:, 34816:35840]]
        nm_col, l_col, rl_col = self.nm_col, self.l_col, self.rl_col
        rmax = self.m1c[:, 0:1]
        v_re = v_full.rearrange("(kc p) d -> p kc d", p=128)
        NKB = 64
        for head in range(16):
            p.wait("sync", spe, spe.n)
            lk = p.dma("sync", H64[:, K10:K10 + 8192], kt_full[head * 128:(head + 1) * 128, :], sld)
            lv = p.dma("sync", M72[:, V10:V10 + 8192].rearrange("p (kc d) -> p kc d", kc=64),
                       v_re[:, :, head * 128:(head + 1) * 128], sld)
            lq = p.dma("sync", qh, qt_my[head * 128:(head + 1) * 128, :], sld)
            p.wait("tensor", sld, lq)
            for qs in range(8):
                p.wait("sync", sve, sve.n)
                lm = p.dma("sync", M72[:, MB0:MB0 + 8192], mask_in[qs, :, :], sld)
                ev = {}
                for kc in range(16):
                    bk = kc % 4
                    if kc >= 4:
                        p.wait("tensor", sve, ev[kc - 4])
                    spe.plan(1)
                    p.on("tensor", lambda h, kc=kc, qs=qs, bk=bk: h.matmul(
                        pb[bk][:, :], qh[:, qs * 128:(qs + 1) * 128],
                        H64[:, K10 + kc * 512:K10 + (kc + 1) * 512],
                        start=True, stop=True).then_inc(spe.h, 1))
                    mmn = spe.n
                    p.wait("vector", spe, mmn)
                    if kc == 0:
                        p.wait("vector", sld, lm)
                    sve.plan(1)
                    p.on("vector", lambda h, kc=kc, bk=bk: h.tensor_add(
                        mf[:, kc * 512:(kc + 1) * 512], pb[bk][:, :],
                        M72[:, MB0 + kc * 512:MB0 + (kc + 1) * 512]).then_inc(sve.h, 1))
                    ev[kc] = sve.n
                sc_done = sve.n
                p.wait("vector", sve, sc_done)
                sve.plan(1)
                p.on("vector", lambda h: h.reduce_max(
                    nm_col[:, :], mf[:, 0:8192], axis=AX.X, negate=True).then_inc(sve.h, 1))
                p.wait("vector", sve, sve.n)
                p.wait("scalar", sve, sve.n)
                sse.plan(1)
                p.on("scalar", lambda h: h.activation(
                    H64[:, PB0:PB0 + 8192], mf[:, 0:8192], AF.Exp, bias=nm_col[:, 0:1],
                    accum_out=l_col[:, 0:1]).then_inc(sse.h, 1))
                p.wait("vector", sse, sse.n)
                sve.plan(1)
                p.on("vector", lambda h: h.reciprocal(rl_col[:, :], l_col[:, :]).then_inc(sve.h, 1))
                p.wait("vector", sve, sve.n)
                sve.plan(1)
                p.on("vector", lambda h: h.tensor_scalar_mul(
                    H64[:, PB0:PB0 + 8192], H64[:, PB0:PB0 + 8192], rl_col[:, 0:1]).then_inc(sve.h, 1))
                pn_done = sve.n
                p.wait("tensor", sve, pn_done)
                tev = {}

                def emit_T(g):
                    bks = (0, 1) if g % 2 == 0 else (2, 3)
                    if g >= 2:
                        p.wait("tensor", sve, tev[g - 2][0])
                        p.wait("tensor", sse, tev[g - 2][1])
                    for i in range(8):
                        kb = g * 8 + i
                        bk = bks[i // 4]
                        spe.plan(1)
                        p.on("tensor", lambda h, kb=kb, bk=bk, i=i: h.matmul(
                            pb[bk][:, (i % 4) * 128:(i % 4) * 128 + 128],
                            H64[:, PB0 + kb * 128:PB0 + (kb + 1) * 128],
                            self.c_idn[:, :], start=True, stop=True).then_inc(spe.h, 1))
                    tg = spe.n
                    p.wait("vector", spe, tg)
                    p.wait("scalar", spe, tg)
                    sve.plan(1)
                    p.on("vector", lambda h, g=g, bks=bks: h.tensor_copy(
                        pt_sb[g % 2][:, 0:512], pb[bks[0]][:, :]).then_inc(sve.h, 1))
                    sse.plan(1)
                    p.on("scalar", lambda h, g=g, bks=bks: h.activation(
                        pt_sb[g % 2][:, 512:1024], pb[bks[1]][:, :], AF.Identity,
                        bias=0.0).then_inc(sse.h, 1))
                    tev[g] = (sve.n, sse.n)

                def emit_AV(g):
                    p.wait("tensor", sve, tev[g][0])
                    p.wait("tensor", sse, tev[g][1])
                    for i in range(8):
                        kb = g * 8 + i
                        spe.plan(1)
                        p.on("tensor", lambda h, kb=kb, g=g, i=i: h.matmul(
                            pb[4][:, 0:128], M72[:, V10 + kb * 128:V10 + (kb + 1) * 128],
                            pt_sb[g % 2][:, i * 128:(i + 1) * 128],
                            start=(kb == 0), stop=(kb == NKB - 1)).then_inc(spe.h, 1))

                emit_T(0)
                for g in range(1, 8):
                    emit_T(g)
                    emit_AV(g - 1)
                emit_AV(7)
                av_done = spe.n
                p.wait("vector", spe, av_done)
                sve.plan(1)
                o = AVT0 + head * 1024 + qs * 128
                p.on("vector", lambda h, o=o: h.tensor_copy(
                    H64[:, o:o + 128], pb[4][:, 0:128]).then_inc(sve.h, 1))
                p.wait("tensor", sve, sve.n)
        p.wait("sync", sve, sve.n)
        return sve.n


"""The three launch programs built from MK builders."""


def _aps(nc, hcn):
    ap = lambda name, shp, dt, kind="ExternalInput": nc.dram_tensor(name, shp, dt, kind=kind).ap()
    CB = {k: ap(k, list(v.shape), BF16 if v.dtype == BF else F32) for k, v in hcn.items()}
    return ap, CB


def build_launch_a(hcn):
    nc = bass.Bass("TRN2", target_bir_lowering=False)
    ap, CB = _aps(nc, {k: hcn[k] for k in
                       ["c_idn", "c_meanw", "c_h128", "c_h64", "c_ones"]})
    xT = ap("xT", [D, T], F32)
    W1 = ap("W1", [H, D], F32)
    W2 = ap("W2", [D, H], F32)
    W3 = ap("W3", [H, D], F32)
    Wz = ap("Wz", [D, D], F32)
    lam_logit = ap("lam_logit", [1, 4], F32)
    eta_in = ap("eta", [1, 4], F32)
    x1T_o = ap("x1T", [D, T], F32, "ExternalOutput")
    hT_o = ap("hT", [D, T], F32, "ExternalOutput")
    S_o = ap("S", [128, 64], F32, "ExternalOutput")
    dr = lambda n, s, dt=F32: nc.dram_tensor(n, s, dt)
    w1bf = dr("w1bf", [H, D], BF16)
    w3bf = dr("w3bf", [H, D], BF16)
    w2bf = dr("w2bf", [D, H], BF16)
    wzbf = dr("wzbf", [D, D], BF16)
    w2a = dr("w2a", [H, D], BF16)
    w2p = dr("w2p", [H, D], BF16)
    lam_d = dr("lam_d", [1, 16])
    with ExitStack() as ctx:
        mk = MK(nc, ctx, CB, skip_consts=("c_ind16", "c_iota1", "c_causal"))
        p = mk.p
        mk.emit_consts()
        w1_done = mk.cast_weight(w1bf, W1, 8)
        w3_done = mk.cast_weight(w3bf, W3, 8)
        w2_done = mk.cast_weight(w2bf, W2, 4)
        wz_done = mk.cast_weight(wzbf, Wz, 2)
        mk.emit_lamprep(lam_logit, eta_in, lam_d, need_ln=False)
        w2p_done = mk.emit_w2prep(w2bf, w2a, w2p, w2_done)
        p.wait("vector", mk.sld, w2p_done)
        p.wait("scalar", mk.sld, w2p_done)
        mk.emit_ln(xT)
        mk.emit_ffn(xT, x1T_o, w1bf, w3bf, w2p, w1_done, w3_done, w2p_done)
        mk.emit_ln(x1T_o)
        mk.emit_zscan(wzbf, wz_done, hT_o, S_o)
        p.wait("sync", mk.sst, mk.sst.n)
        p.emit()
    return nc


def build_launch_b(hcn):
    nc = bass.Bass("TRN2", target_bir_lowering=False)
    ap, CB = _aps(nc, {k: hcn[k] for k in
                       ["c_idn", "c_meanw", "c_h128", "c_h64", "c_ones", "c_iota1", "c_ind16"]})
    x1T = ap("x1T", [D, T], F32)
    hT = ap("hT", [D, T], F32)
    S_all = ap("S_all", [8, 128, 64], F32)
    expo = ap("expo", [1, 8], F32)
    lam_logit = ap("lam_logit", [1, 4], F32)
    eta_in = ap("eta", [1, 4], F32)
    gamR = ap("gamR", [128, 16], F32)
    betR = ap("betR", [128, 16], F32)
    lngR = ap("lngR", [128, 16], F32)
    lnbR = ap("lnbR", [128, 16], F32)
    bqR = ap("bqR", [128, 16], F32)
    bkR = ap("bkR", [128, 16], F32)
    bvR = ap("bvR", [128, 16], F32)
    W1 = ap("W1", [H, D], F32)
    W2 = ap("W2", [D, H], F32)
    W3 = ap("W3", [H, D], F32)
    Wq = ap("Wq", [D, D], F32)
    Wk = ap("Wk", [D, D], F32)
    Wv = ap("Wv", [D, D], F32)
    wa_r = ap("wa_r", [128, 256], F32)
    rg = ap("rg", [16, D, 1024], F32)
    rw1 = ap("rw1", [16, D, 1024], F32)
    rw2 = ap("rw2", [16, 1024, D], F32)
    sg = ap("sg", [2, D, 1024], F32)
    sw1 = ap("sw1", [2, D, 1024], F32)
    sw2 = ap("sw2", [2, 1024, D], F32)
    rb2 = ap("rb2", [16, D], F32)
    sb2 = ap("sb2", [2, D], F32)
    rgb_r = ap("rgb_r", [128, 128], F32)
    rb1_r = ap("rb1_r", [128, 128], F32)
    sgb_r = ap("sgb_r", [128, 16], F32)
    sb1_r = ap("sb1_r", [128, 16], F32)
    x4T_o = ap("x4T", [D, T], F32, "ExternalOutput")
    QT_o = ap("QT", [D, T], BF16, "ExternalOutput")
    KT_o = ap("KT", [D, T], BF16, "ExternalOutput")
    VT_o = ap("VT", [D, T], BF16, "ExternalOutput")
    dr = lambda n, s, dt=F32: nc.dram_tensor(n, s, dt)
    hfix = dr("hfix", [D, T])
    aout = dr("aout", [D, T])
    x2d = dr("x2d", [D, T])
    x3d = dr("x3d", [D, T])
    xln5 = dr("xln5", [D, T])
    lam_d = dr("lam_d", [1, 16])
    w1bf = dr("w1bf", [H, D], BF16)
    w3bf = dr("w3bf", [H, D], BF16)
    w2bf = dr("w2bf", [D, H], BF16)
    w2a = dr("w2a", [H, D], BF16)
    w2p = dr("w2p", [H, D], BF16)
    wqbf = dr("wqbf", [D, D], BF16)
    wkbf = dr("wkbf", [D, D], BF16)
    wvbf = dr("wvbf", [D, D], BF16)
    with ExitStack() as ctx:
        mk = MK(nc, ctx, CB, skip_consts=("c_causal",))
        p = mk.p
        mk.emit_consts()
        w1_done = mk.cast_weight(w1bf, W1, 8)
        w3_done = mk.cast_weight(w3bf, W3, 8)
        w2_done = mk.cast_weight(w2bf, W2, 4)
        wq_done = mk.cast_weight(wqbf, Wq, 4)
        wk_done = mk.cast_weight(wkbf, Wk, 4)
        wv_done = mk.cast_weight(wvbf, Wv, 4)
        mk.emit_lamprep(lam_logit, eta_in, lam_d, need_ln=True)
        mk.emit_carry_fix(hT, hfix.ap(), S_all, expo)
        gam_sb = p.sb([128, 16], F32)
        bet_sb = p.sb([128, 16], F32)
        lng_sb = p.sb([128, 16], F32)
        lnb_sb = p.sb([128, 16], F32)
        bq_sb = p.sb([128, 16], F32)
        bk_sb = p.sb([128, 16], F32)
        bv_sb = p.sb([128, 16], F32)
        for dst, src in [(gam_sb, gamR), (bet_sb, betR), (lng_sb, lngR), (lnb_sb, lnbR),
                         (bq_sb, bqR), (bk_sb, bkR), (bv_sb, bvR)]:
            p.dma("sync", dst[:, :], src[:, :], mk.sld)
        small_ld = mk.sld.n
        p.wait("vector", mk.sld, small_ld)
        mk.sve.plan(1)
        p.on("vector", lambda h: h.tensor_scalar_mul(bq_sb[:, :], bq_sb[:, :], RSQ_DK).then_inc(mk.sve.h, 1))
        mk.emit_ln(hfix.ap(), affine=(gam_sb, bet_sb), out_f32_hbm=aout.ap())
        mk.emit_add(x1T, aout.ap(), x2d.ap())
        mk.emit_ln(x2d.ap())
        w2p_done = mk.emit_w2prep(w2bf, w2a, w2p, w2_done)
        p.wait("vector", mk.sld, w2p_done)
        p.wait("scalar", mk.sld, w2p_done)
        mk.emit_ffn(x2d.ap(), x3d.ap(), w1bf, w3bf, w2p, w1_done, w3_done, w2p_done)
        mk.emit_ln(x3d.ap())
        Wmoe = dict(wa_r=wa_r, rg=rg, rw1=rw1, rw2=rw2, sg=sg, sw1=sw1, sw2=sw2,
                    rb2=rb2, sb2=sb2)
        biasR = dict(rgb_r=rgb_r, rb1_r=rb1_r, sgb_r=sgb_r, sb1_r=sb1_r)
        mk.emit_moe(x3d.ap(), x4T_o, Wmoe, biasR)
        mk.emit_ln(x4T_o, out_f32_hbm=xln5.ap())
        mk.emit_ln(xln5.ap(), affine=(lng_sb, lnb_sb))
        mk.emit_proj(wqbf, wq_done, QT_o, bias_sb=bq_sb, scale=RSQ_DK)
        mk.emit_proj(wkbf, wk_done, KT_o, bias_sb=bk_sb)
        mk.emit_proj(wvbf, wv_done, VT_o, bias_sb=bv_sb)
        p.wait("sync", mk.sst, mk.sst.n)
        p.emit()
    return nc


def build_launch_c(hcn):
    nc = bass.Bass("TRN2", target_bir_lowering=False)
    ap, CB = _aps(nc, {k: hcn[k] for k in ["c_idn", "c_meanw", "c_ind16"]})
    x4T = ap("x4T", [D, T], F32)
    QT = ap("QT", [D, T], BF16)
    KTf = ap("KTf", [D, 8192], BF16)
    Vf = ap("Vf", [8192, D], BF16)
    cmask = ap("cmask", [8, 128, 8192], BF16)
    Wo = ap("Wo", [D, D], F32)
    boR = ap("boR", [128, 16], F32)
    wa_r = ap("wa_r", [128, 256], F32)
    rg = ap("rg", [16, D, 1024], F32)
    rw1 = ap("rw1", [16, D, 1024], F32)
    rw2 = ap("rw2", [16, 1024, D], F32)
    sg = ap("sg", [2, D, 1024], F32)
    sw1 = ap("sw1", [2, D, 1024], F32)
    sw2 = ap("sw2", [2, 1024, D], F32)
    rb2 = ap("rb2", [16, D], F32)
    sb2 = ap("sb2", [2, D], F32)
    rgb_r = ap("rgb_r", [128, 128], F32)
    rb1_r = ap("rb1_r", [128, 128], F32)
    sgb_r = ap("sgb_r", [128, 16], F32)
    sb1_r = ap("sb1_r", [128, 16], F32)
    x6T_o = ap("x6T", [D, T], F32, "ExternalOutput")
    dr = lambda n, s, dt=F32: nc.dram_tensor(n, s, dt)
    x5d = dr("x5d", [D, T])
    wobf = dr("wobf", [D, D], BF16)
    with ExitStack() as ctx:
        mk = MK(nc, ctx, CB, skip_consts=("c_h128", "c_h64", "c_ones", "c_iota1", "c_causal"))
        p = mk.p
        mk.emit_consts()
        wo_done = mk.cast_weight(wobf, Wo, 4)
        bo_sb = p.sb([128, 16], F32)
        p.dma("sync", bo_sb[:, :], boR[:, :], mk.sld)
        mk.emit_attn(QT, KTf, Vf, cmask)
        mk.emit_proj(wobf, wo_done, x5d.ap(), bias_sb=bo_sb,
                     xn_src=mk.H64[:, 8192:24576], add_hbm=x4T)
        mk.emit_ln(x5d.ap())
        Wmoe = dict(wa_r=wa_r, rg=rg, rw1=rw1, rw2=rw2, sg=sg, sw1=sw1, sw2=sw2,
                    rb2=rb2, sb2=sb2)
        biasR = dict(rgb_r=rgb_r, rb1_r=rb1_r, sgb_r=sgb_r, sb1_r=sb1_r)
        mk.emit_moe(x5d.ap(), x6T_o, Wmoe, biasR)
        p.wait("sync", mk.sst, mk.sst.n)
        p.emit()
    return nc


def resh2048(v):
    return np.ascontiguousarray(np.asarray(v).reshape(16, 128).T)


def reshEF(v):
    v = np.asarray(v)
    return np.ascontiguousarray(
        v.reshape(v.shape[0], 8, 128).transpose(2, 0, 1).reshape(128, v.shape[0] * 8))


def host_expo(rr):
    return np.array([[1024.0 * (rr - 1 - j) if j < rr else 1e30 for j in range(8)]],
                    np.float32)


def host_cmask(rr):
    cm = np.zeros((8, 128, 8192), np.float32)
    for s in range(8):
        qg = rr * 1024 + s * 128 + np.arange(128)
        cm[s] = np.where(np.arange(8192)[None, :] <= qg[:, None], 0.0, -1e30)
    return cm.astype(BF)


def run_all(inputs, run_spmd):
    hcn = host_consts()
    z = inputs
    x = np.asarray(z["L0_attn_x"])[0]
    lam_logit = np.asarray(z["L0_attn_lam_logit"]).reshape(1, 4)
    eta = np.asarray(z["L0_attn_eta"]).reshape(1, 4)
    wa_r = np.ascontiguousarray(
        np.asarray(z["L1_ffn_wa"]).reshape(16, 128, 16).transpose(1, 0, 2).reshape(128, 256))

    common_moe = dict(
        wa_r=wa_r, rg=z["L1_ffn_rg"], rw1=z["L1_ffn_rw1"], rw2=z["L1_ffn_rw2"],
        sg=z["L1_ffn_sg"], sw1=z["L1_ffn_sw1"], sw2=z["L1_ffn_sw2"],
        rb2=z["L1_ffn_rb2"], sb2=z["L1_ffn_sb2"],
        rgb_r=reshEF(z["L1_ffn_rgb"]), rb1_r=reshEF(z["L1_ffn_rb1"]),
        sgb_r=reshEF(z["L1_ffn_sgb"]), sb1_r=reshEF(z["L1_ffn_sb1"]))

    hcnA = {k: hcn[k] for k in ["c_idn", "c_meanw", "c_h128", "c_h64", "c_ones"]}
    in_a = []
    for rr in range(NCORE):
        im = dict(xT=np.ascontiguousarray(x[rr * T:(rr + 1) * T].T),
                  W1=z["L0_ffn_W1"], W2=z["L0_ffn_W2"], W3=z["L0_ffn_W3"],
                  Wz=z["L0_attn_W"], lam_logit=lam_logit, eta=eta)
        im.update(hcnA)
        in_a.append(im)
    nc_a = build_launch_a(hcn)
    res_a = run_spmd(nc_a, in_a)

    S_all = np.stack([np.asarray(res_a[rr]["S"]) for rr in range(NCORE)], axis=0)
    hcnB = {k: hcn[k] for k in ["c_idn", "c_meanw", "c_h128", "c_h64", "c_ones",
                                "c_iota1", "c_ind16"]}
    in_b = []
    for rr in range(NCORE):
        im = dict(
            x1T=np.asarray(res_a[rr]["x1T"]), hT=np.asarray(res_a[rr]["hT"]),
            S_all=S_all, expo=host_expo(rr), lam_logit=lam_logit, eta=eta,
            gamR=resh2048(z["L0_attn_gamma"]), betR=resh2048(z["L0_attn_beta"]),
            lngR=resh2048(z["L1_attn_ln_g"]), lnbR=resh2048(z["L1_attn_ln_b"]),
            bqR=resh2048(z["L1_attn_bq"]), bkR=resh2048(z["L1_attn_bk"]),
            bvR=resh2048(z["L1_attn_bv"]),
            W1=z["L0_ffn_W1"], W2=z["L0_ffn_W2"], W3=z["L0_ffn_W3"],
            Wq=z["L1_attn_Wq"], Wk=z["L1_attn_Wk"], Wv=z["L1_attn_Wv"])
        im.update(common_moe)
        im.update(hcnB)
        in_b.append(im)
    nc_b = build_launch_b(hcn)
    res_b = run_spmd(nc_b, in_b)

    KTf = np.concatenate([np.asarray(res_b[rr]["KT"]) for rr in range(NCORE)], axis=1)
    Vf = np.concatenate([np.ascontiguousarray(np.asarray(res_b[rr]["VT"]).T)
                         for rr in range(NCORE)], axis=0)
    hcnC = {k: hcn[k] for k in ["c_idn", "c_meanw", "c_ind16"]}
    in_c = []
    for rr in range(NCORE):
        im = dict(x4T=np.asarray(res_b[rr]["x4T"]), QT=np.asarray(res_b[rr]["QT"]),
                  KTf=np.ascontiguousarray(KTf), Vf=np.ascontiguousarray(Vf),
                  cmask=host_cmask(rr), Wo=z["L1_attn_Wo"], boR=resh2048(z["L1_attn_bo"]))
        im.update(common_moe)
        im.update(hcnC)
        in_c.append(im)
    nc_c = build_launch_c(hcn)
    res_c = run_spmd(nc_c, in_c)

    out = np.concatenate([np.asarray(res_c[rr]["x6T"]).T for rr in range(NCORE)], axis=0)
    return out.reshape(1, 8192, 2048).astype(np.float32)


def kernel(**inputs):
    def run_spmd(nc, in_maps):
        res = run_bass_kernel_spmd(nc, in_maps, core_ids=list(range(NCORE)))
        return res.results
    return run_all(inputs, run_spmd)
